# version 02
# baseline (speedup 1.0000x reference)
import os

import numpy as np
import ml_dtypes

import concourse.bass as bass
import concourse.mybir as mybir
from concourse.bass import IndirectOffsetOnAxis
from concourse.bass_utils import run_bass_kernel_spmd

F32 = mybir.dt.float32
BF16 = mybir.dt.bfloat16
AF = mybir.ActivationFunctionType
ALU = mybir.AluOpType
AX = mybir.AxisListType.X

D = 2048
B = 2
T = 8192
SH = 2048
NKC = 16
TC = 512
NTC = 4
LAMBDA_INIT = 0.8
RMS_EPS = 1.1920929e-07
LN_EPS = 1e-5
FFN_EPS = 1e-5
B_EPS = 1e-6
SCALE_A = float(1.0 / np.sqrt(128.0))

NPH = int(os.environ.get("KERNEL_NPHASES", "99"))
DBG = os.environ.get("KERNEL_DBG", "")
ENG_METHOD = {"T": "tensor", "V": "vector", "A": "scalar", "S": "sync", "P": "gpsimd"}
ENG_ORDER = ["S", "P", "T", "A", "V"]


class Em:

    def __init__(self):
        self.record = True
        self.c = {k: 0 for k in ENG_ORDER + ["C"]}
        self.ev = {}
        self.nc = None
        self.sems = None
        self.eng = None
        self.cur = None

    def start_pass2(self, nc, sems):
        self.record = False
        self.c = {k: 0 for k in ENG_ORDER + ["C"]}
        self.nc = nc
        self.sems = sems
        self.trace = {k: [] for k in ENG_ORDER + ["C", "G"]}

    def bind(self, key, engine):
        self.eng = engine
        self.cur = key

    def do(self, key, f, name=None, waits=(), dma=False, sem_key=None):
        sk = sem_key or key
        step = 16 if dma else 1
        if self.record:
            self.c[sk] += step
            if name is not None:
                assert name not in self.ev, f"dup event {name}"
                self.ev[name] = (sk, self.c[sk])
            return
        assert key == self.cur, (key, self.cur, name)
        rw = []
        if key == "V":
            self.eng.wait_ge(self.sems["V"], self.c["V"])
        for w in waits:
            wk, wv = self.ev[w]
            if wk == key and wv <= self.c[key]:
                continue
            rw.append((wk, wv, w))
            self.eng.wait_ge(self.sems[wk], wv)
        self.trace[key].append((rw, step, name, sk))
        inst = f(self.eng)
        self.c[sk] += step
        if name is not None:
            assert self.ev[name] == (sk, self.c[sk]), f"pass mismatch {name}"
        inst.then_inc(self.sems[sk], step)

    def alias(self, name, target):
        if self.record:
            assert name not in self.ev, name
            self.ev[name] = self.ev[target]


def _tile_w(w):
    K, O = w.shape
    assert O % 128 == 0
    return np.ascontiguousarray(w.reshape(K, O // 128, 128).transpose(1, 0, 2))


def _rope_tables():
    inv = 1.0 / (10000.0 ** (np.arange(0, 128, 2, dtype=np.float64) / 128.0))
    t = np.arange(T, dtype=np.float64)
    fr = np.outer(t, inv)
    emb = np.concatenate([fr, fr], axis=1)
    cos = np.ascontiguousarray(np.cos(emb).astype(np.float32).T)
    sin = np.sin(emb).astype(np.float32).T
    s2 = sin.copy()
    s2[0:64, :] *= -1.0
    return cos, np.ascontiguousarray(s2)


def build_program():
    nc = bass.Bass("TRN2", target_bir_lowering=False)

    def din(name, shape, dtype=F32):
        return nc.dram_tensor(name, list(shape), dtype, kind="ExternalInput").ap()

    x_sh = din("x_sh", [D, SH])
    aqT = din("aqT", [4, D, 128])
    akT = din("akT", [4, D, 128])
    avT = din("avT", [4, D, 128])
    awoT = din("awoT", [16, D, 128])
    lamq1 = din("lamq1", [1, 256])
    lamk1 = din("lamk1", [1, 256])
    lamq2 = din("lamq2", [1, 256])
    lamk2 = din("lamk2", [1, 256])
    bqT = din("bqT", [4, D, 128])
    bkT = din("bkT", [1, D, 128])
    bvT = din("bvT", [1, D, 128])
    bwoT = din("bwoT", [16, D, 128])
    gate_w = din("gate_w", [D, 8])
    upw = din("upw", [64, D, 128])
    dnw = din("dnw", [16, 8192, 128])
    exw = din("exw", [16, D, 128])
    gamc = din("gamc", [128, 16])
    betc = din("betc", [128, 16])
    rcos = din("rcos", [128, T])
    rsin2 = din("rsin2", [128, T])
    yidx = din("yidx", [128, 64], mybir.dt.int32)

    out_fm = nc.dram_tensor("out_fm", [D, SH], F32, kind="ExternalOutput").ap()

    def dint(name, shape, dtype):
        return nc.dram_tensor(name, list(shape), dtype, kind="Internal").ap()

    resid_d = dint("resid_d", [D, SH], F32)
    xln1_loc = dint("xln1_loc", [D, SH], BF16)
    xg1 = dint("xg1", [4 * D, SH], BF16)
    qa_d = dint("qa_d", [512, T], BF16)
    ka_d = dint("ka_d", [512, T], BF16)
    vtm_d = dint("vtm_d", [T, 512], BF16)
    ya_loc = dint("ya_loc", [2048, SH], BF16)
    yag = dint("yag", [4 * 2048, SH], BF16)
    xln3_loc = dint("xln3_loc", [D, SH], BF16)
    xg3 = dint("xg3", [4 * D, SH], BF16)
    yb_loc = dint("yb_loc", [2048, SH], BF16)
    ybg = dint("ybg", [4 * 2048, SH], BF16)
    fence_in = dint("fence_in", [1, 128], F32)
    fence_out = dint("fence_out", [1, 128], F32)
    upb = dint("upb", [64, D, 128], BF16)
    dnb = dint("dnb", [16, 8192, 128], BF16)
    exb = dint("exb", [16, D, 128], BF16)

    DBG_BUFS = {
        "xln1_loc": (xln1_loc, [D, SH], BF16),
        "xg1": (xg1, [4 * D, SH], BF16),
        "qa_d": (qa_d, [512, T], BF16),
        "ka_d": (ka_d, [512, T], BF16),
        "vtm_d": (vtm_d, [T, 512], BF16),
        "ya_loc": (ya_loc, [2048, SH], BF16),
        "yag": (yag, [4 * 2048, SH], BF16),
        "resid_d": (resid_d, [D, SH], F32),
        "xln3_loc": (xln3_loc, [D, SH], BF16),
        "xg3": (xg3, [4 * D, SH], BF16),
        "yb_loc": (yb_loc, [2048, SH], BF16),
        "ybg": (ybg, [4 * 2048, SH], BF16),
    }
    dbg_names = [s for s in DBG.split(",") if s and s in DBG_BUFS]
    dbg_rows = nc.dram_tensor("dbg_rows", [1, 3072], F32, kind="ExternalOutput").ap() if DBG else None
    dbg_lam = nc.dram_tensor("dbg_lam", [128, 1], F32, kind="ExternalOutput").ap() if DBG else None
    dbg_outs = {}
    for nmd in dbg_names:
        buf, shp, dt = DBG_BUFS[nmd]
        dbg_outs[nmd] = (nc.dram_tensor("dbg_" + nmd, shp, dt, kind="ExternalOutput").ap(), buf)

    ident_bf = nc.inline_tensor(np.eye(128, dtype=ml_dtypes.bfloat16), name="c_ident_bf").ap()
    ident_f = nc.inline_tensor(np.eye(128, dtype=np.float32), name="c_ident_f").ap()
    ones_cb_d = nc.inline_tensor(np.ones((128, 1), dtype=ml_dtypes.bfloat16), name="c_ones_cb").ap()
    ones_rf_d = nc.inline_tensor(np.ones((1, 128), dtype=np.float32), name="c_ones_rf").ap()
    tm = np.zeros((128, 1024), dtype=ml_dtypes.bfloat16)
    for k in range(128):
        tm[k, k + 512:] = 1.0
    tmask_d = nc.inline_tensor(tm, name="c_tmask").ap()
    epsv = np.array([[LN_EPS, RMS_EPS, B_EPS, 0.0, 0.0, 0.0, 0.0, 0.0]], dtype=np.float32)
    eps_d = nc.inline_tensor(epsv, name="c_eps").ap()

    sems = {k: nc.alloc_semaphore(f"sem_{k}") for k in ENG_ORDER}
    sems["C"] = nc.alloc_semaphore("sem_C")
    sems["G"] = nc.alloc_semaphore("sem_G")
    em = Em()

    class Hd(dict):
        def __missing__(self, k):
            if em.record:
                return None
            raise KeyError(k)

    H = Hd()
    bar_n = [0]
    RG4 = [[0, 1, 2, 3], [4, 5, 6, 7]]

    def barrier():
        bar_n[0] += 1
        tgt = 5 * bar_n[0]
        if em.record:
            em.c["G"] = tgt
            return
        for k in ENG_ORDER:
            em.trace[k].append((("BARRIER", tgt), None, f"bar{bar_n[0]}"))
        with nc.Block(f"bar{bar_n[0]}") as blk:
            for k in ENG_ORDER:
                def f(e, _t=tgt):
                    e.sem_inc(sems["G"], 1)
                    e.wait_ge(sems["G"], _t)
                getattr(blk, ENG_METHOD[k])(f)

    def run_block(title, parts):
        if em.record:
            for k, fn in parts:
                fn()
            return
        with nc.Block(title) as blk:
            for k, fn in parts:
                def wrap(e, _k=k, _fn=fn):
                    em.bind(_k, e)
                    _fn()
                getattr(blk, ENG_METHOD[k])(wrap)

    def cc_block(name, cin, cout, wait_ev):
        def fn():
            em.do("P", lambda e: e.collective_compute(
                "AllGather", ALU.bypass, replica_groups=RG4,
                ins=[cin.opt()], outs=[cout.opt()]),
                waits=[wait_ev], name=f"cc.{name}", sem_key="C")
            em.do("P", lambda e: e.collective_compute(
                "AllReduce", ALU.add, replica_groups=RG4,
                ins=[fence_in.opt()], outs=[fence_out.opt()]),
                waits=[f"cc.{name}"], name=f"ccf.{name}", sem_key="C")
        run_block(f"cc_{name}", [("P", fn)])

    def p0_sync():
        for nm, src in [("id_bf", ident_bf), ("id_f", ident_f), ("ones_cb", ones_cb_d),
                        ("ones_rf", ones_rf_d), ("tmask", tmask_d), ("gam", gamc),
                        ("bet", betc)]:
            em.do("S", lambda e, nm=nm, src=src: e.dma_start(out=H[nm][:, :], in_=src[:, :]),
                  dma=True)
        em.do("S", lambda e: e.dma_start(out=H["epsc"][:, :], in_=eps_d[:, :]), dma=True)
        em.do("S", lambda e: e.dma_start(out=H["yidx"][:, :], in_=yidx[:, :]), dma=True)
        for i, src in enumerate([lamq1, lamk1, lamq2, lamk2]):
            em.do("S", lambda e, i=i, src=src: e.dma_start(
                out=H["lam4"][0:1, i * 256:(i + 1) * 256], in_=src[:, :]),
                dma=True, name=("p0.lam" if i == 3 else None))

    def p0_gpsimd():
        em.do("P", lambda e: e.dma_start(out=H["fz"][:, :], in_=ones_rf_d[:, :]), dma=True)
        em.do("P", lambda e: e.dma_start(out=fence_in.tensor.ap() if False else fence_in, in_=H["fz"][:, :]), dma=True)
        for kc in range(NKC):
            em.do("P", lambda e, kc=kc: e.dma_start(
                out=H["gate_bf"][:, kc * 8:(kc + 1) * 8],
                in_=gate_w[kc * 128:(kc + 1) * 128, :]),
                dma=True, name=(f"p0.gate" if kc == NKC - 1 else None))

    def p0_vector():
        r = H["rows0"]
        em.do("V", lambda e: e.tensor_mul(r[0:1, 0:256], H["lam4"][0:1, 0:256],
                                          H["lam4"][0:1, 256:512]), waits=["p0.lam"])
        em.do("V", lambda e: e.tensor_mul(r[0:1, 256:512], H["lam4"][0:1, 512:768],
                                          H["lam4"][0:1, 768:1024]))
        em.do("V", lambda e: e.reduce_sum(r[0:1, 512:513], r[0:1, 0:256], axis=AX))
        em.do("V", lambda e: e.reduce_sum(r[0:1, 513:514], r[0:1, 256:512], axis=AX),
              name="p0.dots")
        em.do("V", lambda e: e.tensor_sub(r[0:1, 516:517], r[0:1, 514:515], r[0:1, 515:516]),
              waits=["p0.exps"])
        em.do("V", lambda e: e.tensor_single_scalar(r[0:1, 517:518], r[0:1, 516:517],
                                                    LAMBDA_INIT, ALU.add))
        em.do("V", lambda e: e.tensor_single_scalar(r[0:1, 518:519], r[0:1, 517:518],
                                                    -1.0, ALU.mult), name="p0.lamneg")
        em.do("V", lambda e: e.tensor_copy(H["lamn128"][:, :], H["ps0"][:, 0:1]),
              waits=["p0.lambcast"], name="p0.lamn128")

    def p0_scalar():
        r = H["rows0"]
        em.do("A", lambda e: e.activation(r[0:1, 514:515], r[0:1, 512:513], AF.Exp),
              waits=["p0.dots"])
        em.do("A", lambda e: e.activation(r[0:1, 515:516], r[0:1, 513:514], AF.Exp),
              name="p0.exps")

    def p0_tensor():
        em.do("T", lambda e: e.matmul(H["ps0"][:, 0:1], H["ones_rf"][0:1, :],
                                      H["rows0"][0:1, 518:519], start=True, stop=True),
              waits=["p0.lamneg"], name="p0.lambcast")

    def p1_sync():
        for tc in range(NTC):
            for c in range(NKC):
                w = [f"p1.ap.{tc-1}.{c}"] if tc > 0 else []
                em.do("S", lambda e, tc=tc, c=c: e.dma_start(
                    out=H["rbuf"][:, c, :],
                    in_=x_sh[c * 128:(c + 1) * 128, tc * TC:(tc + 1) * TC]),
                    waits=w, dma=True, name=(f"p1.ld.{tc}" if c == NKC - 1 else None))
            for c in range(NKC):
                em.do("S", lambda e, tc=tc, c=c: e.dma_start(
                    out=xln1_loc[c * 128:(c + 1) * 128, tc * TC:(tc + 1) * TC],
                    in_=H["xln1"][:, c, :]),
                    waits=[f"p1.ap.{tc}.{c}"], dma=True,
                    name=(f"p1.st.{tc}" if c == NKC - 1 else None))

    def p1_tensor():
        for tc in range(NTC):
            for c in range(NKC):
                w = [f"p1.xb.{tc}.{c}"]
                if c == 0 and tc > 0:
                    w.append(f"p1.rowsum.{tc-1}")
                em.do("T", lambda e, c=c: e.matmul(
                    H["ps0"][0:1, :], H["ones_cb"][:, 0:1], H["xbb"][:, c % 2, :],
                    start=(c == 0), stop=(c == NKC - 1)), waits=w)
                em.do("T", lambda e, c=c: e.matmul(
                    H["ps1"][0:1, :], H["ones_cb"][:, 0:1], H["sqb"][:, c % 2, :],
                    start=(c == 0), stop=(c == NKC - 1)),
                    waits=[f"p1.sq.{tc}.{c}"], name=f"p1.mm.{tc}.{c}")
            w = [f"p1.rows.{tc}"]
            if tc > 0:
                w.append(f"p1.ap.{tc-1}.{NKC-1}")
            em.do("T", lambda e: e.matmul(
                H["ps2"][:, :], H["ones_rf"][0:1, :], H["rows0"][0:1, 1024:1536],
                start=True, stop=True), waits=w)
            em.do("T", lambda e: e.matmul(
                H["ps3"][:, :], H["ones_rf"][0:1, :], H["rows0"][0:1, 1536:2048],
                start=True, stop=True), name=f"p1.bc.{tc}")

    def p1_scalar():
        for tc in range(NTC):
            for c in range(NKC):
                w = [f"p1.ld.{tc}"]
                if c >= 2:
                    w.append(f"p1.mm.{tc}.{c-2}")
                elif tc > 0:
                    w.append(f"p1.mm.{tc-1}.{NKC-2+c}")
                em.do("A", lambda e, tc=tc, c=c: e.activation(
                    H["sqb"][:, c % 2, :], H["rbuf"][:, c, :], AF.Square),
                    waits=w, name=f"p1.sq.{tc}.{c}")
            em.do("A", lambda e: e.activation(
                H["rows0"][0:1, 0:512], H["ps0"][0:1, :], AF.Copy, scale=1.0 / D),
                waits=[f"p1.mm.{tc}.{NKC-1}"])
            em.do("A", lambda e: e.activation(
                H["rows0"][0:1, 512:1024], H["ps1"][0:1, :], AF.Copy, scale=1.0 / D),
                name=f"p1.rowsum.{tc}")
            em.do("A", lambda e: e.activation(
                H["rows0"][0:1, 2048:2560], H["rows0"][0:1, 2560:3072], AF.Sqrt,
                bias=H["epsc"][0:1, 0:1]),
                waits=[f"p1.var.{tc}"], name=f"p1.std.{tc}")

    def p1_vector():
        for tc in range(NTC):
            for c in range(NKC):
                w = [f"p1.ld.{tc}"]
                if c >= 2:
                    w.append(f"p1.mm.{tc}.{c-2}")
                elif tc > 0:
                    w.append(f"p1.mm.{tc-1}.{NKC-2+c}")
                em.do("V", lambda e, tc=tc, c=c: e.tensor_copy(
                    H["xbb"][:, c % 2, :], H["rbuf"][:, c, :]),
                    waits=w, name=f"p1.xb.{tc}.{c}")
            r = H["rows0"]
            em.do("V", lambda e: e.tensor_mul(r[0:1, 2560:3072], r[0:1, 0:512], r[0:1, 0:512]),
                  waits=[f"p1.rowsum.{tc}"])
            em.do("V", lambda e: e.tensor_sub(r[0:1, 2560:3072], r[0:1, 512:1024],
                                              r[0:1, 2560:3072]), name=f"p1.var.{tc}")
            em.do("V", lambda e: e.reciprocal(r[0:1, 1024:1536], r[0:1, 2048:2560]),
                  waits=[f"p1.std.{tc}"])
            em.do("V", lambda e: e.tensor_mul(r[0:1, 1536:2048], r[0:1, 0:512],
                                              r[0:1, 1024:1536]), name=f"p1.rows.{tc}")
            for c in range(NKC):
                em.do("V", lambda e, c=c: e.tensor_mul(
                    H["tscr"][:, :], H["rbuf"][:, c, :], H["ps2"][:, :]),
                    waits=([f"p1.bc.{tc}"] if c == 0 else []))
                em.do("V", lambda e, tc=tc, c=c: e.tensor_sub(
                    H["xln1"][:, c, :], H["tscr"][:, :], H["ps3"][:, :]),
                    name=f"p1.ap.{tc}.{c}")

    OCH2 = [("q", i) for i in range(4)] + [("k", i) for i in range(4)] + [("v", i) for i in range(4)]
    WSRC2 = {"q": aqT, "k": akT, "v": avT}
    QKDST = {"q": qa_d, "k": ka_d}
    SCHED2 = []
    _ql = 0
    _vl = 0
    _pl = 0
    for _s in range(4):
        for _oi, (_kind, _o) in enumerate(OCH2):
            for _tc in range(4):
                if _kind == "v":
                    SCHED2.append((_s, _oi, _kind, _o, _tc, _s * 4 + _tc, _pl, None, _vl))
                    _vl += 1
                else:
                    SCHED2.append((_s, _oi, _kind, _o, _tc, _s * 4 + _tc, _pl, _ql, None))
                    _ql += 1
                _pl += 1
    S2BY_PL = {e[6]: e for e in SCHED2}

    def p2_sync():
        done_xb = set()
        for (s, oi, kind, o, tc, g, pl, ql, vl) in SCHED2:
            if s not in done_xb:
                done_xb.add(s)
                for kc in range(NKC):
                    w = ["ccf.xg1"] if (pl == 0 and kc == 0) else []
                    if pl > 0 and kc == 0:
                        w.append(f"p2.mm.{pl - 1}")
                    em.do("S", lambda e, s=s, kc=kc: e.dma_start(
                        out=H["xbuf"][:, kc, :],
                        in_=xg1[s * D + kc * 128: s * D + (kc + 1) * 128, :]),
                        waits=w, dma=True,
                        name=(f"p2.xb.{s}" if kc == NKC - 1 else None))
            if kind in ("q", "k"):
                em.do("S", lambda e, kind=kind, o=o, g=g, ql=ql: e.dma_start(
                    out=QKDST[kind][o * 128:(o + 1) * 128, g * TC:(g + 1) * TC],
                    in_=H["qn"][:, ql % 2, :]),
                    waits=[f"p2.qn.{ql}"], dma=True, name=f"p2.qkst.{ql}")
            else:
                for ti in range(4):
                    em.do("S", lambda e, o=o, g=g, ti=ti, vl=vl: e.dma_start(
                        out=vtm_d[g * TC + ti * 128:g * TC + (ti + 1) * 128,
                                  o * 128:(o + 1) * 128],
                        in_=H["vstg"][:, (vl % 2) * 4 + ti, :]),
                        waits=[f"p2.vt.{vl}.{ti}"], dma=True,
                        name=(f"p2.vst.{vl}" if ti == 3 else None))

    def p2_gpsimd():
        seen = []
        for (s, oi, kind, o, tc, g, pl, ql, vl) in SCHED2:
            if (s, oi) in seen:
                continue
            idx = len(seen)
            seen.append((s, oi))
            w = []
            if idx >= 2:
                w.append(f"p2.mm.{(idx - 2) * 4 + 3}")
            em.do("P", lambda e, kind=kind, o=o, idx=idx: e.dma_start(
                out=H["wsl"][:, idx % 2, :, :],
                in_=WSRC2[kind][o].rearrange("(kc p) c -> p kc c", p=128)),
                waits=w, dma=True, name=f"p2.w.{s}.{oi}")

    def p2_tensor():
        for (s, oi, kind, o, tc, g, pl, ql, vl) in SCHED2:
            for kc in range(NKC):
                w = []
                if kc == 0:
                    w = [f"p2.w.{s}.{oi}", f"p2.xb.{s}"]
                    if pl >= 2:
                        w.append(f"p2.ev.{pl-2}")
                em.do("T", lambda e, tc=tc, kc=kc, pl=pl: e.matmul(
                    H["psp"][pl % 2][:, :], H["wsl"][:, (pl // 4) % 2, kc, :],
                    H["xbuf"][:, kc, tc * TC:(tc + 1) * TC],
                    start=(kc == 0), stop=(kc == NKC - 1)),
                    waits=w, name=(f"p2.mm.{pl}" if kc == NKC - 1 else None))
            if kind in ("q", "k"):
                w = [f"p2.sq.{ql}"]
                if ql > 0:
                    w.append(f"p2.std.{ql-1}")
                em.do("T", lambda e, ql=ql: e.matmul(
                    H["ps0"][0:1, :], H["ones_cb"][:, 0:1], H["sqb"][:, ql % 2, :],
                    start=True, stop=True), waits=w, name=f"p2.ssq.{ql}")
                w = [f"p2.row.{ql}"]
                if ql >= 1:
                    w.append(f"p2.qn.{ql-1}")
                em.do("T", lambda e, ql=ql: e.matmul(
                    H["ps2"][:, :], H["ones_rf"][0:1, :],
                    H["rows0"][0:1, 1024 + (ql % 2) * 512:1536 + (ql % 2) * 512],
                    start=True, stop=True), waits=w, name=f"p2.bc.{ql}")
            else:
                for ti in range(4):
                    gti = vl * 4 + ti
                    w = []
                    if ti == 0:
                        w.append(f"p2.vb.{vl}")
                    if gti >= 2:
                        pvl, pti = divmod(gti - 2, 4)
                        w.append(f"p2.vt.{pvl}.{pti}")
                    em.do("T", lambda e, vl=vl, ti=ti: e.transpose(
                        H["pst2"][ti % 2][:, (ti // 2) * 128:(ti // 2 + 1) * 128],
                        H["vbf"][:, vl % 2, ti * 128:(ti + 1) * 128], H["id_bf"][:, :]),
                        waits=w, name=f"p2.tr.{vl}.{ti}")

    def p2_scalar():
        for (s, oi, kind, o, tc, g, pl, ql, vl) in SCHED2:
            if kind in ("q", "k"):
                w = [f"p2.mm.{pl}"]
                if ql >= 2:
                    w.append(f"p2.ssq.{ql-2}")
                em.do("A", lambda e, ql=ql, pl=pl: e.activation(
                    H["sqb"][:, ql % 2, :], H["psp"][pl % 2][:, :], AF.Square),
                    waits=w, name=f"p2.sq.{ql}")
                w2 = []
                if ql >= 2:
                    w2.append(f"p2.qn.{ql-2}")
                em.do("A", lambda e, ql=ql, pl=pl: e.activation(
                    H["qf32"][:, ql % 2, :], H["psp"][pl % 2][:, :], AF.Copy),
                    waits=w2, name=f"p2.qc.{ql}")
                em.alias(f"p2.ev.{pl}", f"p2.qc.{ql}")
                em.do("A", lambda e, ql=ql: e.activation(
                    H["rows0"][0:1, (ql % 2) * 512:(ql % 2) * 512 + 512],
                    H["ps0"][0:1, :], AF.Sqrt, scale=1.0 / 128.0,
                    bias=H["epsc"][0:1, 1:2]),
                    waits=[f"p2.ssq.{ql}"], name=f"p2.std.{ql}")
            else:
                w = [f"p2.mm.{pl}"]
                if vl >= 2:
                    w.append(f"p2.tr.{vl-1}.3")
                em.do("A", lambda e, vl=vl, pl=pl: e.activation(
                    H["vbf"][:, vl % 2, :], H["psp"][pl % 2][:, :], AF.Copy),
                    waits=w, name=f"p2.vb.{vl}")
                em.alias(f"p2.ev.{pl}", f"p2.vb.{vl}")

    def p2_vector():
        for (s, oi, kind, o, tc, g, pl, ql, vl) in SCHED2:
            if kind in ("q", "k"):
                em.do("V", lambda e, ql=ql: e.reciprocal(
                    H["rows0"][0:1, 1024 + (ql % 2) * 512:1536 + (ql % 2) * 512],
                    H["rows0"][0:1, (ql % 2) * 512:(ql % 2) * 512 + 512]),
                    waits=[f"p2.std.{ql}"], name=f"p2.row.{ql}")
                w = [f"p2.bc.{ql}", f"p2.qc.{ql}"]
                if ql >= 2:
                    w.append(f"p2.qkst.{ql-2}")
                em.do("V", lambda e, ql=ql: e.tensor_mul(
                    H["qn"][:, ql % 2, :], H["qf32"][:, ql % 2, :], H["ps2"][:, :]),
                    waits=w, name=f"p2.qn.{ql}")
            else:
                for ti in range(4):
                    w = [f"p2.tr.{vl}.{ti}"]
                    if vl >= 2:
                        w.append(f"p2.vst.{vl-2}")
                    em.do("V", lambda e, vl=vl, ti=ti: e.tensor_copy(
                        H["vstg"][:, (vl % 2) * 4 + ti, :],
                        H["pst2"][ti % 2][:, (ti // 2) * 128:(ti // 2 + 1) * 128]),
                        waits=w, name=f"p2.vt.{vl}.{ti}")

    IT3 = []
    for _qc in range(16):
        for _h in range(2):
            ext = 4 * (_qc + 1)
            for _kb in range(ext):
                IT3.append((_qc, _h, _kb, _kb == 0, _kb == ext - 1, _kb >= 4 * _qc))
    NIT3 = len(IT3)
    GRP3 = []
    _i = 0
    for _qc in range(16):
        for _h in range(2):
            _i += 4 * (_qc + 1)
            GRP3.append((_qc, _h, _i - 1))

    def p3_gpsimd():
        for i in range(8):
            em.do("P", lambda e, i=i: e.dma_start(
                out=upb[i * 8:(i + 1) * 8], in_=upw[i * 8:(i + 1) * 8]), dma=True,
                name=("cache.up" if i == 7 else None))
        for i in range(8):
            em.do("P", lambda e, i=i: e.dma_start(
                out=dnb[i * 2:(i + 1) * 2], in_=dnw[i * 2:(i + 1) * 2]), dma=True,
                name=("cache.dn" if i == 7 else None))
        for i in range(4):
            em.do("P", lambda e, i=i: e.dma_start(
                out=exb[i * 4:(i + 1) * 4], in_=exw[i * 4:(i + 1) * 4]), dma=True,
                name=("cache.ex" if i == 3 else None))

    def p3_sync():
        for ch in range(4):
            em.do("S", lambda e, ch=ch: e.dma_start(
                out=H["Qsb"][:, ch, :], in_=qa_d[ch * 128:(ch + 1) * 128, :]),
                waits=[f"p2.vst.{_vl-1}"], dma=True)
            em.do("S", lambda e, ch=ch: e.dma_start(
                out=H["Ksb"][:, ch, :], in_=ka_d[ch * 128:(ch + 1) * 128, :]),
                dma=True, name=("p3.qk" if ch == 3 else None))
        git = 0
        for it, (qc, h, kb, first, last, diag) in enumerate(IT3):
            w = [] if it >= 2 else []
            if it >= 2:
                w = [f"p3.av.{it-2}"]
            em.do("S", lambda e, kb=kb, h=h, it=it: e.dma_start(
                out=H["Vt"][:, it % 2, :],
                in_=vtm_d[kb * 128:(kb + 1) * 128, h * 256:(h + 1) * 256]),
                waits=w, dma=True, name=f"p3.v.{it}")
            if last:
                qc_, h_, _ = GRP3[git]
                for vd in range(2):
                    r0 = (qc // 4) * 512 + h * 256 + vd * 128
                    c0 = (qc % 4) * TC
                    em.do("S", lambda e, r0=r0, c0=c0, vd=vd: e.dma_start(
                        out=ya_loc[r0:r0 + 128, c0:c0 + TC],
                        in_=H["ycb"][:, vd, :]),
                        waits=[f"p3.yc.{git}.{vd}"], dma=True,
                        name=(f"p3.yst.{git}" if vd == 1 else None))
                git += 1

    def p3_tensor():
        git = 0
        for it, (qc, h, kb, first, last, diag) in enumerate(IT3):
            w = []
            if it == 0:
                w = ["p3.qk"]
            else:
                w = [f"p3.at.{it-1}"]
            if first and it > 0:
                w += [f"p3.epiv.{git-1}", f"p3.epia.{git-1}"]
            em.do("T", lambda e, h=h, qc=qc, kb=kb: e.matmul(
                H["ps0"][:, :], H["Ksb"][:, h * 2, kb * 128:(kb + 1) * 128],
                H["Qsb"][:, h * 2, qc * TC:(qc + 1) * TC], start=True, stop=True),
                waits=w)
            em.do("T", lambda e, h=h, qc=qc, kb=kb: e.matmul(
                H["ps1"][:, :], H["Ksb"][:, h * 2 + 1, kb * 128:(kb + 1) * 128],
                H["Qsb"][:, h * 2 + 1, qc * TC:(qc + 1) * TC], start=True, stop=True),
                name=f"p3.sc.{it}")
            atw = f"p3.atm.{it}" if diag else f"p3.at.{it}"
            for half in range(2):
                for vd in range(2):
                    em.do("T", lambda e, half=half, vd=vd, it=it, first=first, last=last: e.matmul(
                        H["psy"][half * 2 + vd][:, :],
                        H["Vt"][:, it % 2, vd * 128:(vd + 1) * 128],
                        H["at"][:, it % 2, half * TC:(half + 1) * TC],
                        start=first, stop=last),
                        waits=([atw, f"p3.v.{it}"] if (half == 0 and vd == 0) else []))
                em.do("T", lambda e, half=half, it=it, first=first, last=last: e.matmul(
                    H["psd"][half][0:1, :], H["ones_cb"][:, 0:1],
                    H["at"][:, it % 2, half * TC:(half + 1) * TC],
                    start=first, stop=last),
                    name=(f"p3.av.{it}" if half == 1 else None))
            if last:
                em.do("T", lambda e: e.matmul(
                    H["ps0"][:, :], H["ones_rf"][0:1, :], H["rows0"][0:1, 0:512],
                    start=True, stop=True),
                    waits=[f"p3.at.{it}", f"p3.rec.{git}"])
                em.do("T", lambda e: e.matmul(
                    H["ps1"][:, :], H["ones_rf"][0:1, :], H["rows0"][0:1, 512:1024],
                    start=True, stop=True), name=f"p3.bc.{git}")
                git += 1

    def p3_scalar():
        git = 0
        for it, (qc, h, kb, first, last, diag) in enumerate(IT3):
            dst = "atr" if diag else "at"
            w = [f"p3.sc.{it}"]
            if it >= 2:
                w.append(f"p3.av.{it-2}")
            em.do("A", lambda e, it=it, dst=dst: e.activation(
                H[dst][:, it % 2, 0:512], H["ps0"][:, :], AF.Exp, scale=SCALE_A),
                waits=w)
            em.do("A", lambda e, it=it, dst=dst: e.activation(
                H[dst][:, it % 2, 512:1024], H["ps1"][:, :], AF.Exp, scale=SCALE_A),
                name=f"p3.at.{it}")
            if last:
                for k in range(4):
                    em.do("A", lambda e, k=k: e.activation(
                        H["yf"][:, k, :], H["psy"][k][:, :], AF.Copy),
                        waits=([f"p3.av.{it}"] if k == 0 else []),
                        name=(f"p3.epia.{git}" if k == 3 else None))
                git += 1

    def p3_vector():
        git = 0
        for it, (qc, h, kb, first, last, diag) in enumerate(IT3):
            if diag:
                o = (kb - 4 * qc) * 128
                for half in range(2):
                    w = []
                    if half == 0:
                        w.append(f"p3.at.{it}")
                        if it >= 2:
                            w.append(f"p3.av.{it-2}")
                    em.do("V", lambda e, it=it, half=half, o=o: e.tensor_mul(
                        H["at"][:, it % 2, half * TC:(half + 1) * TC],
                        H["atr"][:, it % 2, half * TC:(half + 1) * TC],
                        H["tmask"][:, 512 - o:1024 - o]),
                        waits=w,
                        name=(f"p3.atm.{it}" if half == 1 else None))
            if last:
                r = H["rows0"]
                em.do("V", lambda e: e.reciprocal(r[0:1, 0:512], H["psd"][0][0:1, :]),
                      waits=[f"p3.av.{it}"])
                em.do("V", lambda e: e.reciprocal(r[0:1, 512:1024], H["psd"][1][0:1, :]))
                em.do("V", lambda e: e.tensor_scalar_mul(
                    r[0:1, 512:1024], r[0:1, 512:1024], H["lamn128"][0:1, 0:1]),
                    name=f"p3.rec.{git}")
                for vd in range(2):
                    w = []
                    if vd == 0:
                        w = [f"p3.bc.{git}", f"p3.epia.{git}"]
                        if git > 0:
                            w.append(f"p3.yst.{git-1}")
                    em.do("V", lambda e, vd=vd: e.tensor_mul(
                        H["yg"][:, 0, :], H["yf"][:, vd, :], H["ps0"][:, :]), waits=w)
                    em.do("V", lambda e, vd=vd: e.tensor_mul(
                        H["yg"][:, 1, :], H["yf"][:, 2 + vd, :], H["ps1"][:, :]))
                    em.do("V", lambda e, vd=vd: e.tensor_add(
                        H["ycb"][:, vd, :], H["yg"][:, 0, :], H["yg"][:, 1, :]),
                        name=f"p3.yc.{git}.{vd}")
                em.alias(f"p3.epiv.{git}", f"p3.yc.{git}.1")
                git += 1

    def ln_events(P, tag, tc):
        return {
            "mm": lambda c: f"{P}.{tag}mm.{tc}.{c}",
            "xb": lambda c: f"{P}.{tag}xb.{tc}.{c}",
            "sq": lambda c: f"{P}.{tag}sq.{tc}.{c}",
            "row": f"{P}.{tag}row.{tc}",
            "var": f"{P}.{tag}var.{tc}",
            "std": f"{P}.{tag}std.{tc}",
            "rows": f"{P}.{tag}rows.{tc}",
            "bc": f"{P}.{tag}bc.{tc}",
        }

    def ln_T(E, ps_free_w, bc_free_w):
        for c in range(NKC):
            w = [E["xb"](c)]
            if c == 0:
                w += ps_free_w
            em.do("T", lambda e, c=c: e.matmul(
                H["ps0"][0:1, :], H["ones_cb"][:, 0:1], H["sqb2"][:, 0, c % 2, :],
                start=(c == 0), stop=(c == NKC - 1)), waits=w)
            em.do("T", lambda e, c=c: e.matmul(
                H["ps1"][0:1, :], H["ones_cb"][:, 0:1], H["sqb2"][:, 1, c % 2, :],
                start=(c == 0), stop=(c == NKC - 1)),
                waits=[E["sq"](c)], name=E["mm"](c))
        em.do("T", lambda e: e.matmul(
            H["ps2"][:, :], H["ones_rf"][0:1, :], H["rows0"][0:1, 1024:1536],
            start=True, stop=True), waits=[E["rows"]] + bc_free_w)
        em.do("T", lambda e: e.matmul(
            H["ps3"][:, :], H["ones_rf"][0:1, :], H["rows0"][0:1, 1536:2048],
            start=True, stop=True), name=E["bc"])

    def ln_A(E, src, eps, ready, prev_mm):
        for c in range(NKC):
            w = [ready(c)]
            if c >= 2:
                w.append(E["mm"](c - 2))
            elif prev_mm is not None:
                w.append(prev_mm(NKC - 2 + c))
            em.do("A", lambda e, c=c: e.activation(
                H["sqb2"][:, 1, c % 2, :], src(c), AF.Square),
                waits=w, name=E["sq"](c))
        em.do("A", lambda e: e.activation(
            H["rows0"][0:1, 0:512], H["ps0"][0:1, :], AF.Copy, scale=1.0 / D),
            waits=[E["mm"](NKC - 1)])
        em.do("A", lambda e: e.activation(
            H["rows0"][0:1, 512:1024], H["ps1"][0:1, :], AF.Copy, scale=1.0 / D),
            name=E["row"])
        em.do("A", lambda e: e.activation(
            H["rows0"][0:1, 2048:2560], H["rows0"][0:1, 2560:3072], AF.Sqrt,
            bias=H["epsc"][0:1, 0:1]),
            waits=[E["var"]], name=E["std"])

    def ln_V_pre(E, src, ready, prev_mm):
        for c in range(NKC):
            w = [ready(c)]
            if c >= 2:
                w.append(E["mm"](c - 2))
            elif prev_mm is not None:
                w.append(prev_mm(NKC - 2 + c))
            em.do("V", lambda e, c=c: e.tensor_copy(
                H["sqb2"][:, 0, c % 2, :], src(c)),
                waits=w, name=E["xb"](c))
        r = H["rows0"]
        em.do("V", lambda e: e.tensor_mul(r[0:1, 2560:3072], r[0:1, 0:512], r[0:1, 0:512]),
              waits=[E["row"]])
        em.do("V", lambda e: e.tensor_sub(r[0:1, 2560:3072], r[0:1, 512:1024],
                                          r[0:1, 2560:3072]), name=E["var"])
        em.do("V", lambda e: e.reciprocal(r[0:1, 1024:1536], r[0:1, 2048:2560]),
              waits=[E["std"]])
        em.do("V", lambda e: e.tensor_mul(r[0:1, 1536:2048], r[0:1, 0:512],
                                          r[0:1, 1024:1536]), name=E["rows"])

    def p47(part):
        P = f"p4{part}"
        gath = yag if part == 0 else ybg
        wsrc = awoT if part == 0 else bwoT
        wscale = (1.0 - LAMBDA_INIT) if part == 0 else 1.0
        ccwait = "ccf.yag" if part == 0 else "ccf.ybg"
        last_tag = "ln3" if part == 0 else "ln2"

        def rows_of_kc(kc):
            return (kc // 4) * 512 + (kc % 4) * 128

        def h2c(uc):
            return H["h2"][:, uc * 512:(uc + 1) * 512]

        def wosl(par, kc):
            return H["h2"][:, par * 2048 + kc * 128: par * 2048 + (kc + 1) * 128]

        def exsl(par, kc):
            return H["h2"][:, 4096 + par * 2048 + kc * 128: 4096 + par * 2048 + (kc + 1) * 128]

        def wosl_dst(par):
            return H["h2"][:, par * 2048:(par + 1) * 2048].rearrange("p (kc c) -> p kc c", c=128)

        def exsl_dst(par):
            return H["h2"][:, 4096 + par * 2048:4096 + (par + 1) * 2048].rearrange(
                "p (kc c) -> p kc c", c=128)

        def xlnc(c):
            return H["xlnbuf"][:, c * 512:(c + 1) * 512]

        def dnsl(hh, kk):
            return H["xlnbuf"][:, hh * 4096 + kk * 128: hh * 4096 + (kk + 1) * 128]

        def dnsl_dst(hh):
            return H["xlnbuf"][:, hh * 4096:(hh + 1) * 4096].rearrange("p (kc c) -> p kc c", c=128)

        def fsync():
            for tc in range(NTC):
                src = x_sh if part == 0 else resid_d
                for c in range(NKC):
                    w = []
                    if part == 1 and tc == 0 and c == 0:
                        w += [f"p40.rst.{k}" for k in range(NTC)]
                    if tc > 0 and c == 0:
                        w.append(f"{P}.{last_tag}apl.{tc-1}")
                        w.append(f"{P}.rst.{tc-1}")
                        if part == 1:
                            w.append(f"{P}.ost.{tc-1}")
                    em.do("S", lambda e, tc=tc, c=c, src=src: e.dma_start(
                        out=H["resid"][:, c, :],
                        in_=src[c * 128:(c + 1) * 128, tc * TC:(tc + 1) * TC]),
                        waits=w, dma=True, name=(f"{P}.rld.{tc}" if c == NKC - 1 else None))
                for c in range(NKC):
                    em.do("S", lambda e, tc=tc, c=c: e.dma_start(
                        out=resid_d[c * 128:(c + 1) * 128, tc * TC:(tc + 1) * TC],
                        in_=H["resid"][:, c, :]),
                        waits=[f"{P}.r2.{tc}.{c}"], dma=True,
                        name=(f"{P}.rst.{tc}" if c == NKC - 1 else None))
                if part == 0:
                    for c in range(NKC):
                        em.do("S", lambda e, tc=tc, c=c: e.dma_start(
                            out=xln3_loc[c * 128:(c + 1) * 128, tc * TC:(tc + 1) * TC],
                            in_=H["ybuf2"][:, c, :]),
                            waits=[f"{P}.ln3ap.{tc}.{c}"], dma=True,
                            name=(f"{P}.xst.{tc}" if c == NKC - 1 else None))
                else:
                    for c in range(NKC):
                        em.do("S", lambda e, tc=tc, c=c: e.dma_start(
                            out=out_fm[c * 128:(c + 1) * 128, tc * TC:(tc + 1) * TC],
                            in_=H["resid"][:, c, :]),
                            waits=[f"{P}.r2.{tc}.{c}"], dma=True,
                            name=(f"{P}.ost.{tc}" if c == NKC - 1 else None))

        def fgpsimd():
            for tc in range(NTC):
                for kc in range(NKC):
                    w = []
                    if tc == 0 and kc == 0:
                        w.append(ccwait)
                    if tc > 0 and kc == 0:
                        w.append(f"{P}.womm.{tc-1}.15")
                        if part == 0:
                            w.append(f"{P}.xst.{tc-1}")
                        else:
                            w.append(f"{P}.ost.{tc-1}")
                    em.do("P", lambda e, tc=tc, kc=kc: e.indirect_dma_start(
                        H["ybuf2"][:, kc, :], None,
                        gath.rearrange("r (a c) -> (r a) c", c=TC),
                        IndirectOffsetOnAxis(H["yidx"][:, kc * 4 + tc:kc * 4 + tc + 1], 0)),
                        waits=w, dma=True, name=(f"{P}.yb.{tc}" if kc == NKC - 1 else None))
                for o in range(16):
                    idx = tc * 16 + o
                    w = []
                    if idx >= 2:
                        ptc, po = divmod(idx - 2, 16)
                        w.append(f"{P}.womm.{ptc}.{po}")
                    if tc > 0 and o < 2:
                        w.append(f"{P}.dnmm.{tc-1}.15.1")
                    em.do("P", lambda e, o=o, w=w: e.dma_start(
                        out=wosl_dst(o % 2), in_=wsrc[o].rearrange("(kc p) c -> p kc c", p=128)),
                        waits=w, dma=True, name=f"{P}.wo.{tc}.{o}")
                for o in range(16):
                    idx = tc * 16 + o
                    w = []
                    if part == 0 and tc == 0 and o == 0:
                        w.append("cache.ex")
                    if idx >= 2:
                        ptc, po = divmod(idx - 2, 16)
                        w.append(f"{P}.exmm.{ptc}.{po}")
                    if tc > 0 and o < 2:
                        w.append(f"{P}.dnmm.{tc-1}.15.1")
                    em.do("P", lambda e, o=o, w=w: e.dma_start(
                        out=exsl_dst(o % 2), in_=exb[o].rearrange("(kc p) c -> p kc c", p=128)),
                        waits=w, dma=True, name=f"{P}.ex.{tc}.{o}")
                for uc in range(64):
                    idx = tc * 64 + uc
                    w = []
                    if part == 0 and tc == 0 and uc == 0:
                        w.append("cache.up")
                    if idx >= 2:
                        ptc, pu = divmod(idx - 2, 64)
                        w.append(f"{P}.upmm.{ptc}.{pu}")
                    em.do("P", lambda e, uc=uc, w=w: e.dma_start(
                        out=H["upsl"][:, uc % 2, :, :],
                        in_=upb[uc].rearrange("(kc p) c -> p kc c", p=128)),
                        waits=w, dma=True, name=f"{P}.up.{tc}.{uc}")
                for o in range(16):
                    for hh in range(2):
                        idx = (tc * 16 + o) * 2 + hh
                        w = []
                        if part == 0 and tc == 0 and o == 0 and hh == 0:
                            w.append("cache.dn")
                        if o == 0 and hh == 0:
                            w.append(f"{P}.upmm.{tc}.63")
                        if idx >= 2:
                            q2, ph = divmod(idx - 2, 2)
                            ptc2, po3 = divmod(q2, 16)
                            w.append(f"{P}.dnmm.{ptc2}.{po3}.{ph}")
                        em.do("P", lambda e, o=o, hh=hh, w=w: e.dma_start(
                            out=dnsl_dst(hh),
                            in_=dnb[o, hh * 4096:(hh + 1) * 4096, :].rearrange(
                                "(kc p) c -> p kc c", p=128)),
                            waits=w, dma=True, name=f"{P}.dn.{tc}.{o}.{hh}")

        def ftensor():
            for tc in range(NTC):
                for o in range(16):
                    for kc in range(NKC):
                        w = []
                        if kc == 0:
                            w = [f"{P}.wo.{tc}.{o}", f"{P}.yb.{tc}"]
                            if tc > 0 and o < 2:
                                w.append(f"{P}.y.{tc-1}.{14+o}")
                            idx = tc * 16 + o
                            if idx >= 2:
                                ptc, po = divmod(idx - 2, 16)
                                w.append(f"{P}.r1.{ptc}.{po}")
                        em.do("T", lambda e, o=o, kc=kc, tc=tc: e.matmul(
                            H["psp"][o % 2][:, :], wosl(o % 2, kc),
                            H["ybuf2"][:, kc, :],
                            start=(kc == 0), stop=(kc == NKC - 1)),
                            waits=w, name=(f"{P}.womm.{tc}.{o}" if kc == NKC - 1 else None))
                EA = ln_events(P, "ln", tc)
                psw = []
                if tc > 0:
                    psw.append(f"{P}.{last_tag}row.{tc-1}")
                bcw = []
                ln_T(EA, psw, bcw)
                for ti in range(4):
                    for kc in range(NKC):
                        w = []
                        if ti == 0:
                            w.append(f"{P}.lnap.{tc}.{kc}")
                        if kc == 0 and ti > 0:
                            w.append(f"{P}.glog.{tc}.{ti-1}")
                        if kc == 0 and ti == 0 and tc > 0:
                            w.append(f"{P}.glog.{tc-1}.3")
                        em.do("T", lambda e, ti=ti, kc=kc: e.matmul(
                            H["psg"][0:128, 0:8], xlnc(kc)[:, ti * 128:(ti + 1) * 128],
                            H["gate_bf"][:, kc * 8:(kc + 1) * 8],
                            start=(kc == 0), stop=(kc == NKC - 1)),
                            waits=w, name=(f"{P}.gmm.{tc}.{ti}" if kc == NKC - 1 else None))
                    w = [f"{P}.top2.{tc}.{ti}"]
                    if ti == 0 and tc > 0:
                        w.append(f"{P}.srow.{tc-1}")
                    em.do("T", lambda e, ti=ti: e.transpose(
                        H["pss"][0:1, ti * 128:(ti + 1) * 128],
                        H["scl_c"][:, ti:ti + 1], H["id_f"][:, :]),
                        waits=w, name=f"{P}.scT.{tc}.{ti}")
                em.do("T", lambda e: e.matmul(
                    H["ps3"][:, :], H["ones_rf"][0:1, :], H["rows0"][0:1, 2048:2560],
                    start=True, stop=True),
                    waits=[f"{P}.srow.{tc}"], name=f"{P}.sbc.{tc}")
                for o in range(16):
                    for kc in range(NKC):
                        w = []
                        if kc == 0:
                            w = [f"{P}.ex.{tc}.{o}"]
                            if o < 2:
                                w.append(f"{P}.r1.{tc}.{14+o}")
                            else:
                                w.append(f"{P}.exev.{tc}.{o-2}")
                        em.do("T", lambda e, o=o, kc=kc: e.matmul(
                            H["psp"][o % 2][:, :], exsl(o % 2, kc), xlnc(kc),
                            start=(kc == 0), stop=(kc == NKC - 1)),
                            waits=w, name=(f"{P}.exmm.{tc}.{o}" if kc == NKC - 1 else None))
                for uc in range(64):
                    for kc in range(NKC):
                        w = []
                        if kc == 0:
                            w = [f"{P}.up.{tc}.{uc}"]
                            if uc < 2:
                                w.append(f"{P}.exev.{tc}.{14+uc}")
                            else:
                                w.append(f"{P}.upev.{tc}.{uc-2}")
                        em.do("T", lambda e, uc=uc, kc=kc: e.matmul(
                            H["psp"][uc % 2][:, :], H["upsl"][:, uc % 2, kc, :],
                            xlnc(kc),
                            start=(kc == 0), stop=(kc == NKC - 1)),
                            waits=w, name=(f"{P}.upmm.{tc}.{uc}" if kc == NKC - 1 else None))
                for o in range(16):
                    for hh in range(2):
                        for kk in range(32):
                            w = []
                            if kk == 0:
                                w = [f"{P}.dn.{tc}.{o}.{hh}"]
                                if hh == 0:
                                    if o < 2:
                                        w.append(f"{P}.upev.{tc}.{62+o}")
                                        w.append(f"{P}.upev.{tc}.{63}")
                                    else:
                                        w.append(f"{P}.y.{tc}.{o-2}")
                            em.do("T", lambda e, o=o, hh=hh, kk=kk: e.matmul(
                                H["psp"][o % 2][:, :], dnsl(hh, kk), h2c(hh * 32 + kk),
                                start=(hh == 0 and kk == 0), stop=(hh == 1 and kk == 31)),
                                waits=w,
                                name=(f"{P}.dnmm.{tc}.{o}.{hh}" if kk == 31 else None))
                EB = ln_events(P, "ln2", tc)
                ln_T(EB, [f"{P}.lnrow.{tc}"], [f"{P}.sclb.{tc}"])
                if part == 0:
                    EC = ln_events(P, "ln3", tc)
                    ln_T(EC, [f"{P}.ln2row.{tc}"], [])

        def fscalar():
            for tc in range(NTC):
                EA = ln_events(P, "ln", tc)
                prev = None
                if tc > 0:
                    prev = (lambda c, tc=tc: f"{P}.{last_tag}mm.{tc-1}.{c}")
                ln_A(EA, lambda c: H["resid"][:, c, :], LN_EPS,
                     lambda c: f"{P}.r1.{tc}.{c}", prev)
                for uc in range(64):
                    w = [f"{P}.upmm.{tc}.{uc}"]
                    if uc >= 2:
                        w.append(f"{P}.upev.{tc}.{uc-2}")
                    elif tc > 0:
                        w.append(f"{P}.upev.{tc-1}.{62+uc}")
                    em.do("A", lambda e, uc=uc: e.activation(
                        H["rl"][:, uc % 2, :], H["psp"][uc % 2][:, :], AF.Relu),
                        waits=w, name=f"{P}.rl.{tc}.{uc}")
                EB = ln_events(P, "ln2", tc)
                ln_A(EB, lambda c: H["ymoe"][:, c, :], FFN_EPS,
                     lambda c: f"{P}.y.{tc}.{c}",
                     lambda c: f"{P}.lnmm.{tc}.{c}")
                if part == 0:
                    EC = ln_events(P, "ln3", tc)
                    ln_A(EC, lambda c: H["resid"][:, c, :], LN_EPS,
                         lambda c: f"{P}.r2.{tc}.{c}",
                         lambda c: f"{P}.ln2mm.{tc}.{c}")

        def fvector():
            for tc in range(NTC):
                r = H["rows0"]
                for o in range(16):
                    em.do("V", lambda e, o=o, wscale=wscale: e.scalar_tensor_tensor(
                        H["resid"][:, o, :], H["psp"][o % 2][:, :], wscale,
                        H["resid"][:, o, :], ALU.mult, ALU.add),
                        waits=[f"{P}.womm.{tc}.{o}", f"{P}.rld.{tc}"],
                        name=f"{P}.r1.{tc}.{o}")
                EA = ln_events(P, "ln", tc)
                prev = None
                if tc > 0:
                    prev = (lambda c, tc=tc: f"{P}.{last_tag}mm.{tc-1}.{c}")
                ln_V_pre(EA, lambda c: H["resid"][:, c, :],
                         lambda c: f"{P}.r1.{tc}.{c}", prev)
                for c in range(NKC):
                    w = []
                    if c == 0:
                        w.append(EA["bc"])
                        if tc > 0:
                            w.append(f"{P}.dnmm.{tc-1}.15.1")
                    em.do("V", lambda e, c=c: e.tensor_mul(
                        H["tscr"][:, :], H["resid"][:, c, :], H["ps2"][:, :]), waits=w)
                    em.do("V", lambda e, c=c: e.tensor_sub(
                        xlnc(c), H["tscr"][:, :], H["ps3"][:, :]),
                        name=f"{P}.lnap.{tc}.{c}")
                for ti in range(4):
                    g = H["glog"]
                    em.do("V", lambda e: e.tensor_copy(g[:, 0:8], H["psg"][0:128, 0:8]),
                          waits=[f"{P}.gmm.{tc}.{ti}"], name=f"{P}.glog.{tc}.{ti}")
                    em.do("V", lambda e: e.tensor_max(g[:, 8:12], g[:, 0:4], g[:, 4:8]))
                    em.do("V", lambda e: e.tensor_tensor(g[:, 12:16], g[:, 0:4], g[:, 4:8], ALU.min))
                    em.do("V", lambda e: e.tensor_max(g[:, 16:18], g[:, 8:10], g[:, 10:12]))
                    em.do("V", lambda e: e.tensor_tensor(g[:, 18:20], g[:, 8:10], g[:, 10:12], ALU.min))
                    em.do("V", lambda e: e.tensor_max(g[:, 20:22], g[:, 12:14], g[:, 14:16]))
                    em.do("V", lambda e: e.tensor_max(g[:, 22:24], g[:, 18:20], g[:, 20:22]))
                    em.do("V", lambda e: e.tensor_max(g[:, 24:25], g[:, 16:17], g[:, 17:18]))
                    em.do("V", lambda e: e.tensor_tensor(g[:, 25:26], g[:, 16:17], g[:, 17:18], ALU.min))
                    em.do("V", lambda e: e.tensor_max(g[:, 26:27], g[:, 22:23], g[:, 23:24]))
                    em.do("V", lambda e: e.tensor_max(g[:, 27:28], g[:, 25:26], g[:, 26:27]))
                    em.do("V", lambda e, ti=ti: e.tensor_add(
                        H["scl_c"][:, ti:ti + 1], g[:, 24:25], g[:, 27:28]),
                        name=f"{P}.top2.{tc}.{ti}")
                em.do("V", lambda e: e.tensor_copy(r[0:1, 2048:2560], H["pss"][0:1, :]),
                      waits=[f"{P}.scT.{tc}.3"], name=f"{P}.srow.{tc}")
                em.do("V", lambda e: e.tensor_copy(H["sclb"][:, :], H["ps3"][:, :]),
                      waits=[f"{P}.sbc.{tc}"], name=f"{P}.sclb.{tc}")
                for o in range(16):
                    em.do("V", lambda e, o=o: e.tensor_mul(
                        H["ymoe"][:, o, :], H["psp"][o % 2][:, :], H["sclb"][:, :]),
                        waits=[f"{P}.exmm.{tc}.{o}", f"{P}.sclb.{tc}"],
                        name=f"{P}.exev.{tc}.{o}")
                for uc in range(64):
                    w = [f"{P}.rl.{tc}.{uc}"]
                    if uc == 0:
                        w.append(f"{P}.exmm.{tc}.{15}")
                        w.append(f"{P}.gmm.{tc}.{3}")
                    em.do("V", lambda e, uc=uc: e.tensor_mul(
                        h2c(uc), H["rl"][:, uc % 2, :], H["rl"][:, uc % 2, :]),
                        waits=w, name=f"{P}.upev.{tc}.{uc}")
                for o in range(16):
                    em.do("V", lambda e, o=o: e.tensor_add(
                        H["ymoe"][:, o, :], H["psp"][o % 2][:, :], H["ymoe"][:, o, :]),
                        waits=[f"{P}.dnmm.{tc}.{o}.1"], name=f"{P}.y.{tc}.{o}")
                EB = ln_events(P, "ln2", tc)
                ln_V_pre(EB, lambda c: H["ymoe"][:, c, :],
                         lambda c: f"{P}.y.{tc}.{c}",
                         lambda c: f"{P}.lnmm.{tc}.{c}")
                for c in range(NKC):
                    em.do("V", lambda e, c=c: e.tensor_mul(
                        H["tscr"][:, :], H["ymoe"][:, c, :], H["ps2"][:, :]),
                        waits=([EB["bc"]] if c == 0 else []))
                    em.do("V", lambda e, c=c: e.tensor_sub(
                        H["tscr"][:, :], H["tscr"][:, :], H["ps3"][:, :]))
                    em.do("V", lambda e, c=c: e.scalar_tensor_tensor(
                        H["tscr2"][:, :], H["tscr"][:, :], H["gam"][:, c:c + 1],
                        H["resid"][:, c, :], ALU.mult, ALU.add))
                    em.do("V", lambda e, c=c: e.tensor_scalar(
                        H["resid"][:, c, :], H["tscr2"][:, :], H["bet"][:, c:c + 1],
                        None, ALU.add), name=f"{P}.r2.{tc}.{c}")
                if part == 1:
                    em.alias(f"{P}.ln2apl.{tc}", f"{P}.r2.{tc}.{NKC-1}")
                if part == 0:
                    EC = ln_events(P, "ln3", tc)
                    ln_V_pre(EC, lambda c: H["resid"][:, c, :],
                             lambda c: f"{P}.r2.{tc}.{c}",
                             lambda c: f"{P}.ln2mm.{tc}.{c}")
                    for c in range(NKC):
                        w = []
                        if c == 0:
                            w.append(EC["bc"])
                            w.append(f"{P}.womm.{tc}.15")
                        em.do("V", lambda e, c=c: e.tensor_mul(
                            H["tscr"][:, :], H["resid"][:, c, :], H["ps2"][:, :]), waits=w)
                        em.do("V", lambda e, c=c: e.tensor_sub(
                            H["ybuf2"][:, c, :], H["tscr"][:, :], H["ps3"][:, :]),
                            name=f"{P}.ln3ap.{tc}.{c}")
                    em.alias(f"{P}.ln3apl.{tc}", f"{P}.ln3ap.{tc}.{NKC-1}")

        return [("S", fsync), ("P", fgpsimd), ("T", ftensor), ("A", fscalar), ("V", fvector)]

    OCH5 = [("q", i) for i in range(4)] + [("k", 0), ("v", 0)]
    WSRC5 = {"q": bqT, "k": bkT, "v": bvT}
    SCHED5 = []
    _pl5 = 0
    _rl5 = 0
    _tl5 = 0
    for _s in range(4):
        for _hf in range(2):
            for _oi, (_kind, _o) in enumerate(OCH5):
                for _tc in range(2):
                    g5 = _s * 4 + _hf * 2 + _tc
                    if _kind == "v":
                        SCHED5.append((_s, _hf, _oi, _kind, _o, _tc, g5, _pl5, None, _tl5))
                        _tl5 += 1
                    elif _kind == "k":
                        SCHED5.append((_s, _hf, _oi, _kind, _o, _tc, g5, _pl5, _rl5, _tl5))
                        _rl5 += 1
                        _tl5 += 1
                    else:
                        SCHED5.append((_s, _hf, _oi, _kind, _o, _tc, g5, _pl5, _rl5, None))
                        _rl5 += 1
                    _pl5 += 1
    NPL5 = _pl5

    def p5_sync():
        done_xb = set()
        for (s, hf, oi, kind, o, tc, g, pl, rl, tl) in SCHED5:
            key = (s, hf)
            if key not in done_xb:
                done_xb.add(key)
                for kc in range(NKC):
                    w = []
                    if s == 0 and hf == 0 and kc == 0:
                        w.append("ccf.xg3")
                    if (s, hf) != (0, 0) and kc == 0:
                        w.append(f"p5.mm.{pl-1}")
                    em.do("S", lambda e, s=s, hf=hf, kc=kc: e.dma_start(
                        out=H["xbuf5"][:, kc, :],
                        in_=xg3[s * D + kc * 128: s * D + (kc + 1) * 128,
                                hf * 1024:(hf + 1) * 1024]),
                        waits=w, dma=True,
                        name=(f"p5.xb.{s}.{hf}" if kc == NKC - 1 else None))
            if kind in ("q", "k"):
                w = [f"p5.cosdone.{rl-1}"] if rl > 0 else []
                em.do("S", lambda e, g=g: e.dma_start(
                    out=H["cosb"][:, 0:512], in_=rcos[:, g * TC:(g + 1) * TC]),
                    waits=w, dma=True)
                em.do("S", lambda e, g=g: e.dma_start(
                    out=H["cosb"][:, 512:1024], in_=rsin2[:, g * TC:(g + 1) * TC]),
                    dma=True, name=f"p5.rope.{rl}")

    def p5_gpsimd():
        seen = set()
        for (s, hf, oi, kind, o, tc, g, pl, rl, tl) in SCHED5:
            key = (s, hf, oi)
            if key in seen:
                continue
            seen.add(key)
            idx = (s * 2 + hf) * 6 + oi
            w = []
            if idx >= 2:
                w.append(f"p5.mm.{(idx - 2) * 2 + 1}")
            em.do("P", lambda e, kind=kind, o=o, oi=oi: e.dma_start(
                out=H["wsl5"][:, oi % 2, :, :],
                in_=WSRC5[kind][o].rearrange("(kc p) c -> p kc c", p=128)),
                waits=w, dma=True, name=f"p5.w.{s}.{hf}.{oi}")

    def p5_tensor():
        for (s, hf, oi, kind, o, tc, g, pl, rl, tl) in SCHED5:
            for kc in range(NKC):
                w = []
                if kc == 0:
                    w = [f"p5.w.{s}.{hf}.{oi}", f"p5.xb.{s}.{hf}"]
                    if pl >= 2:
                        w.append(f"p5.ev.{pl-2}")
                em.do("T", lambda e, oi=oi, tc=tc, kc=kc, pl=pl: e.matmul(
                    H["psp"][pl % 2][:, :], H["wsl5"][:, oi % 2, kc, :],
                    H["xbuf5"][:, kc, tc * TC:(tc + 1) * TC],
                    start=(kc == 0), stop=(kc == NKC - 1)),
                    waits=w, name=(f"p5.mm.{pl}" if kc == NKC - 1 else None))
            if kind == "k":
                for ti in range(4):
                    gti = tl * 4 + ti
                    w = [f"p5.ev.{pl}"]
                    if gti >= 2:
                        ptl, pti = divmod(gti - 2, 4)
                        w.append(f"p5.tv.{ptl}.{pti}")
                    em.do("T", lambda e, g=g, ti=ti: e.transpose(
                        H["pst2"][ti % 2][:, (ti // 2) * 128:(ti // 2 + 1) * 128],
                        H["kf"][:, 0, g * TC + ti * 128:g * TC + (ti + 1) * 128],
                        H["id_bf"][:, :]),
                        waits=w, name=f"p5.tr.{tl}.{ti}")
            elif kind == "v":
                for ti in range(4):
                    gti = tl * 4 + ti
                    w = [f"p5.ev.{pl}"]
                    if gti >= 2:
                        ptl, pti = divmod(gti - 2, 4)
                        w.append(f"p5.tv.{ptl}.{pti}")
                    em.do("T", lambda e, ti=ti: e.transpose(
                        H["pst2"][ti % 2][:, (ti // 2) * 128:(ti // 2 + 1) * 128],
                        H["vbf5"][:, ti * 128:(ti + 1) * 128], H["id_bf"][:, :]),
                        waits=w, name=f"p5.tr.{tl}.{ti}")

    def p5_scalar():
        for (s, hf, oi, kind, o, tc, g, pl, rl, tl) in SCHED5:
            if kind == "v":
                w = [f"p5.mm.{pl}"]
                if tl > 0:
                    w.append(f"p5.tr.{tl-1}.3")
                em.do("A", lambda e, pl=pl: e.activation(
                    H["vbf5"][:, :], H["psp"][pl % 2][:, :], AF.Copy),
                    waits=w, name=f"p5.ev.{pl}")
            else:
                em.do("A", lambda e: e.activation(
                    H["scr5"][:, 2, :], H["scr5"][:, 1, :], AF.Exp),
                    waits=[f"p5.min.{rl}"], name=f"p5.exp.{rl}")

    def p5_vector():
        for (s, hf, oi, kind, o, tc, g, pl, rl, tl) in SCHED5:
            if kind == "v":
                for ti in range(4):
                    em.do("V", lambda e, g=g, ti=ti: e.tensor_copy(
                        H["vtm"][:, g * 4 + ti, :],
                        H["pst2"][ti % 2][:, (ti // 2) * 128:(ti // 2 + 1) * 128]),
                        waits=[f"p5.tr.{tl}.{ti}"], name=f"p5.tv.{tl}.{ti}")
                continue
            em.do("V", lambda e, pl=pl: e.tensor_mul(
                H["scr5"][:, 0, :], H["psp"][pl % 2][:, :], H["cosb"][:, 0:512]),
                waits=[f"p5.mm.{pl}", f"p5.rope.{rl}"])
            em.do("V", lambda e, pl=pl: e.tensor_mul(
                H["scr5"][0:64, 1, :], H["psp"][pl % 2][64:128, :],
                H["cosb"][0:64, 512:1024]))
            em.do("V", lambda e, pl=pl: e.tensor_mul(
                H["scr5"][64:128, 1, :], H["psp"][pl % 2][0:64, :],
                H["cosb"][64:128, 512:1024]), name=f"p5.cosdone.{rl}")
            em.do("V", lambda e: e.tensor_add(
                H["scr5"][:, 0, :], H["scr5"][:, 0, :], H["scr5"][:, 1, :]))
            em.do("V", lambda e: e.tensor_single_scalar(
                H["scr5"][:, 1, :], H["scr5"][:, 0, :], 0.0, ALU.min),
                name=f"p5.min.{rl}")
            em.do("V", lambda e: e.tensor_relu(H["scr5"][:, 0, :], H["scr5"][:, 0, :]))
            dst = H["qf"] if kind == "q" else H["kf"]
            ch = o if kind == "q" else 0
            em.do("V", lambda e, dst=dst, ch=ch, g=g: e.tensor_add(
                dst[:, ch, g * TC:(g + 1) * TC], H["scr5"][:, 0, :], H["scr5"][:, 2, :]),
                waits=[f"p5.exp.{rl}"], name=f"p5.ev.{pl}")
            if kind == "k":
                for ti in range(4):
                    em.do("V", lambda e, g=g, ti=ti, tl=tl: e.tensor_copy(
                        H["kftm"][:, g * 4 + ti, :],
                        H["pst2"][ti % 2][:, (ti // 2) * 128:(ti // 2 + 1) * 128]),
                        waits=[f"p5.tr.{tl}.{ti}"], name=f"p5.tv.{tl}.{ti}")

    def gh_prev(ci, h):
        return (ci, h - 1) if h > 0 else (ci - 1, 3)

    def p6_sync():
        for ci in range(16):
            for h in range(4):
                r0 = (ci // 4) * 512 + h * 128
                c0 = (ci % 4) * TC
                em.do("S", lambda e, ci=ci, h=h, r0=r0, c0=c0: e.dma_start(
                    out=yb_loc[r0:r0 + 128, c0:c0 + TC],
                    in_=H["ybb"][:, (ci * 4 + h) % 2, :]),
                    waits=[f"p6.ye.{ci}.{h}"], dma=True, name=f"p6.yst.{ci}.{h}")

    def p6_tensor():
        for ci in range(16):
            for h in range(4):
                lin = ci * 4 + h
                w = []
                if lin == 0:
                    w = ["p6.sinit", f"p5.tv.{_tl5-1}.3"]
                else:
                    pci, ph = gh_prev(ci, h)
                    w = [f"p6.ycp.{pci}.{ph}"]
                    if h == 0:
                        w.append(f"p6.sup.{ci-1}")
                em.do("T", lambda e, ci=ci, h=h: e.matmul(
                    H["psy6"][:, :], H["Sbf"][:, :], H["qf"][:, h, ci * TC:(ci + 1) * TC],
                    start=True, stop=False), waits=w)
                w = []
                if lin > 0:
                    pci, ph = gh_prev(ci, h)
                    w = [f"p6.den.{pci}.{ph}"]
                    if h == 0:
                        w.append(f"p6.supd.{ci-1}")
                em.do("T", lambda e, ci=ci, h=h: e.matmul(
                    H["psd6"][0:1, :], H["zbf"][:, 0:1], H["qf"][:, h, ci * TC:(ci + 1) * TC],
                    start=True, stop=False), waits=w)
                for si in range(4):
                    scl = lin * 4 + si
                    w = []
                    if scl >= 2:
                        w.append(f"p6.am.{scl-2}")
                    em.do("T", lambda e, ci=ci, h=h, si=si: e.matmul(
                        H["pss6"][si % 2][:, :],
                        H["kf"][:, 0, ci * TC + si * 128:ci * TC + (si + 1) * 128],
                        H["qf"][:, h, ci * TC:(ci + 1) * TC], start=True, stop=True),
                        waits=w, name=f"p6.sc.{scl}")
                    em.do("T", lambda e, ci=ci, si=si: e.matmul(
                        H["psy6"][:, :], H["vtm"][:, ci * 4 + si, :],
                        H["at6"][:, si % 2, :], start=False, stop=(si == 3)),
                        waits=[f"p6.am.{scl}"])
                    em.do("T", lambda e, si=si: e.matmul(
                        H["psd6"][0:1, :], H["ones_cb"][:, 0:1],
                        H["at6"][:, si % 2, :], start=False, stop=(si == 3)),
                        name=(f"p6.num.{ci}.{h}" if si == 3 else f"p6.avd.{scl}"))
                w = [f"p6.rec.{ci}.{h}"]
                if lin > 0:
                    pci, ph = gh_prev(ci, h)
                    w.append(f"p6.ye.{pci}.{ph}")
                em.do("T", lambda e: e.matmul(
                    H["psb6"][:, :], H["ones_rf"][0:1, :], H["rows0"][0:1, 512:1024],
                    start=True, stop=True), waits=w, name=f"p6.bc.{ci}.{h}")
            for si in range(4):
                w = []
                if si == 0:
                    w = [f"p6.sup.{ci-1}"] if ci > 0 else ["p6.sinit"]
                em.do("T", lambda e, ci=ci, si=si: e.matmul(
                    H["psS"][:, 0:128], H["kftm"][:, ci * 4 + si, :], H["vtm"][:, ci * 4 + si, :],
                    start=(si == 0), stop=(si == 3)),
                    waits=w, name=(f"p6.dS.{ci}" if si == 3 else None))

    def p6_scalar():
        for ci in range(16):
            for h in range(4):
                lin = ci * 4 + h
                w = [f"p6.num.{ci}.{h}"]
                if lin > 0:
                    pci, ph = gh_prev(ci, h)
                    w.append(f"p6.rec.{pci}.{ph}")
                em.do("A", lambda e: e.activation(
                    H["rows0"][0:1, 0:512], H["psd6"][0:1, :], AF.Copy),
                    waits=w, name=f"p6.den.{ci}.{h}")
                w = []
                if lin > 0:
                    pci, ph = gh_prev(ci, h)
                    w.append(f"p6.ye.{pci}.{ph}")
                em.do("A", lambda e: e.activation(
                    H["yf6"][:, :], H["psy6"][:, :], AF.Copy),
                    waits=w, name=f"p6.ycp.{ci}.{h}")

    def p6_vector():
        em.do("V", lambda e: e.memset(H["Sf"][:, :], 0.0), waits=["p5.lastev"])
        em.do("V", lambda e: e.memset(H["Sbf"][:, :], 0.0))
        em.do("V", lambda e: e.memset(H["zf"][:, :], 0.0))
        em.do("V", lambda e: e.memset(H["zbf"][:, :], 0.0), name="p6.sinit")
        for ci in range(16):
            for h in range(4):
                lin = ci * 4 + h
                for si in range(4):
                    scl = lin * 4 + si
                    o = si * 128
                    w = [f"p6.sc.{scl}"]
                    if scl >= 2:
                        s2_ = scl - 2
                        if s2_ % 4 != 3:
                            w.append(f"p6.avd.{s2_}")
                        else:
                            lin2 = s2_ // 4
                            w.append(f"p6.num.{lin2 // 4}.{lin2 % 4}")
                    em.do("V", lambda e, si=si, o=o: e.tensor_mul(
                        H["at6"][:, si % 2, :], H["pss6"][si % 2][:, :],
                        H["tmask"][:, 512 - o:1024 - o]),
                        waits=w, name=f"p6.am.{scl}")
                em.do("V", lambda e: e.tensor_single_scalar(
                    H["rows0"][0:1, 1024:1536], H["rows0"][0:1, 0:512], B_EPS, ALU.add),
                    waits=[f"p6.den.{ci}.{h}"])
                em.do("V", lambda e: e.reciprocal(
                    H["rows0"][0:1, 512:1024], H["rows0"][0:1, 1024:1536]),
                    name=f"p6.rec.{ci}.{h}")
                w = [f"p6.bc.{ci}.{h}", f"p6.ycp.{ci}.{h}"]
                if lin >= 2:
                    pl2 = lin - 2
                    w.append(f"p6.yst.{pl2 // 4}.{pl2 % 4}")
                em.do("V", lambda e, ci=ci, h=h: e.tensor_mul(
                    H["ybb"][:, (ci * 4 + h) % 2, :], H["yf6"][:, :], H["psb6"][:, :]),
                    waits=w, name=f"p6.ye.{ci}.{h}")
            em.do("V", lambda e, ci=ci: e.tensor_add(H["Sf"][:, :], H["Sf"][:, :], H["psS"][:, 0:128]),
                  waits=[f"p6.dS.{ci}", f"p6.num.{ci}.3"])
            em.do("V", lambda e: e.tensor_copy(H["Sbf"][:, :], H["Sf"][:, :]),
                  name=f"p6.sup.{ci}")
            em.do("V", lambda e, ci=ci: e.reduce_sum(
                H["zscr"][:, 0:1], H["kf"][:, 0, ci * TC:(ci + 1) * TC], axis=AX))
            em.do("V", lambda e: e.tensor_add(H["zf"][:, :], H["zf"][:, :], H["zscr"][:, 0:1]))
            em.do("V", lambda e: e.tensor_copy(H["zbf"][:, :], H["zf"][:, :]),
                  name=f"p6.supd.{ci}")

    def scaffold():
        from contextlib import ExitStack
        p2_ = not em.record

        def fin(evname):
            if em.record:
                return
            with nc.Block("fin") as blk:
                def fw(e):
                    wk, wv = em.ev[evname]
                    e.wait_ge(sems[wk], wv)
                    n = 0
                    for nmd, (oap, buf) in dbg_outs.items():
                        e.dma_start(out=oap, in_=buf).then_inc(sems["S"], 16)
                        n += 1
                    if DBG:
                        e.dma_start(out=dbg_rows, in_=H["rows0"][0:1, :]).then_inc(sems["S"], 16)
                        e.dma_start(out=dbg_lam, in_=H["lamn128"][:, :]).then_inc(sems["S"], 16)
                        n += 2
                    if n:
                        e.wait_ge(sems["S"], em.c["S"] + 16 * n)
                blk.sync(fw)
        with ExitStack() as st:
            if p2_:
                H["id_bf"] = st.enter_context(nc.sbuf_tensor("id_bf", [128, 128], BF16))
                H["id_f"] = st.enter_context(nc.sbuf_tensor("id_f", [128, 128], F32))
                H["ones_cb"] = st.enter_context(nc.sbuf_tensor("ones_cb", [128, 1], BF16))
                H["ones_rf"] = st.enter_context(nc.sbuf_tensor("ones_rf", [1, 128], F32))
                H["tmask"] = st.enter_context(nc.sbuf_tensor("tmask", [128, 1024], BF16))
                H["gam"] = st.enter_context(nc.sbuf_tensor("gam", [128, 16], F32))
                H["bet"] = st.enter_context(nc.sbuf_tensor("bet", [128, 16], F32))
                H["gate_bf"] = st.enter_context(nc.sbuf_tensor("gate_bf", [128, 128], BF16))
                H["lam4"] = st.enter_context(nc.sbuf_tensor("lam4", [1, 1024], F32))
                H["lamn128"] = st.enter_context(nc.sbuf_tensor("lamn128", [128, 1], F32))
                H["rows0"] = st.enter_context(nc.sbuf_tensor("rows0", [1, 3072], F32))
                H["tscr"] = st.enter_context(nc.sbuf_tensor("tscr", [128, 512], F32))
                H["tscr2"] = st.enter_context(nc.sbuf_tensor("tscr2", [128, 512], F32))
                H["epsc"] = st.enter_context(nc.sbuf_tensor("epsc", [1, 8], F32))
                H["fz"] = st.enter_context(nc.sbuf_tensor("fz", [1, 128], F32))
                H["yidx"] = st.enter_context(nc.sbuf_tensor("yidx_sb", [128, 64], mybir.dt.int32))

            with ExitStack() as s01:
                if p2_:
                    H["ps0"] = s01.enter_context(nc.psum_tensor("ps0a", [128, 512], F32))
                    H["ps1"] = s01.enter_context(nc.psum_tensor("ps1a", [128, 512], F32))
                    H["ps2"] = s01.enter_context(nc.psum_tensor("ps2a", [128, 512], F32))
                    H["ps3"] = s01.enter_context(nc.psum_tensor("ps3a", [128, 512], F32))
                run_block("p0", [("S", p0_sync), ("P", p0_gpsimd), ("T", p0_tensor),
                                 ("A", p0_scalar), ("V", p0_vector)])
                barrier()
                if NPH < 1:
                    fin("p0.lamn128"); return
                with ExitStack() as s1:
                    if p2_:
                        H["rbuf"] = s1.enter_context(nc.sbuf_tensor("rbuf", [128, 16, 512], F32))
                        H["xln1"] = s1.enter_context(nc.sbuf_tensor("xln1", [128, 16, 512], BF16))
                        H["xbb"] = s1.enter_context(nc.sbuf_tensor("xbb", [128, 2, 512], BF16))
                        H["sqb"] = s1.enter_context(nc.sbuf_tensor("sqb", [128, 2, 512], BF16))
                    run_block("p1", [("S", p1_sync), ("T", p1_tensor), ("A", p1_scalar),
                                     ("V", p1_vector)])
                    barrier()
            if NPH < 2:
                fin("p1.st.3"); return
            cc_block("xg1", xln1_loc, xg1, "p1.st.3")
            if NPH < 3:
                fin("cc.xg1"); return

            with ExitStack() as s2:
                if p2_:
                    H["xbuf"] = s2.enter_context(nc.sbuf_tensor("xbuf", [128, 16, 2048], BF16))
                    H["wsl"] = s2.enter_context(nc.sbuf_tensor("wsl", [128, 2, 16, 128], BF16))
                    H["qn"] = s2.enter_context(nc.sbuf_tensor("qn", [128, 2, 512], BF16))
                    H["qf32"] = s2.enter_context(nc.sbuf_tensor("qf32", [128, 2, 512], F32))
                    H["vbf"] = s2.enter_context(nc.sbuf_tensor("vbf", [128, 2, 512], BF16))
                    H["vstg"] = s2.enter_context(nc.sbuf_tensor("vstg", [128, 8, 128], BF16))
                    H["sqb"] = s2.enter_context(nc.sbuf_tensor("sqb_p2", [128, 2, 512], BF16))
                    H["ps0"] = s2.enter_context(nc.psum_tensor("ps0b", [128, 512], F32))
                    H["ps2"] = s2.enter_context(nc.psum_tensor("ps2b", [128, 512], F32))
                    H["psp"] = [s2.enter_context(nc.psum_tensor(f"pspb{i}", [128, 512], F32))
                                for i in range(2)]
                    H["pst2"] = [s2.enter_context(nc.psum_tensor(f"pstb{i}", [128, 1024], BF16))
                                 for i in range(2)]
                run_block("p2", [("S", p2_sync), ("P", p2_gpsimd), ("T", p2_tensor),
                                 ("A", p2_scalar), ("V", p2_vector)])
                barrier()
            if NPH < 4:
                fin(f"p2.vst.63"); return
            with ExitStack() as s3:
                if p2_:
                    H["Qsb"] = s3.enter_context(nc.sbuf_tensor("Qsb", [128, 4, 8192], BF16))
                    H["Ksb"] = s3.enter_context(nc.sbuf_tensor("Ksb", [128, 4, 8192], BF16))
                    H["Vt"] = s3.enter_context(nc.sbuf_tensor("Vt", [128, 2, 256], BF16))
                    H["at"] = s3.enter_context(nc.sbuf_tensor("at", [128, 2, 1024], BF16))
                    H["atr"] = s3.enter_context(nc.sbuf_tensor("atr", [128, 2, 1024], BF16))
                    H["yf"] = s3.enter_context(nc.sbuf_tensor("yf", [128, 4, 512], F32))
                    H["yg"] = s3.enter_context(nc.sbuf_tensor("yg", [128, 2, 512], F32))
                    H["ycb"] = s3.enter_context(nc.sbuf_tensor("ycb", [128, 2, 512], BF16))
                    H["ps0"] = s3.enter_context(nc.psum_tensor("ps0c", [128, 512], F32))
                    H["ps1"] = s3.enter_context(nc.psum_tensor("ps1c", [128, 512], F32))
                    H["psy"] = [s3.enter_context(nc.psum_tensor(f"psyc{i}", [128, 512], F32))
                                for i in range(4)]
                    H["psd"] = [s3.enter_context(nc.psum_tensor(f"psdc{i}", [128, 512], F32))
                                for i in range(2)]
                run_block("p3", [("S", p3_sync), ("P", p3_gpsimd), ("T", p3_tensor),
                                 ("A", p3_scalar), ("V", p3_vector)])
                barrier()
            if NPH < 5:
                fin("p3.yst.31"); return
            cc_block("yag", ya_loc, yag, "p3.yst.31")

            def ffn_scope(stx, sfx):
                H["resid"] = stx.enter_context(nc.sbuf_tensor("resid" + sfx, [128, 16, 512], F32))
                H["ymoe"] = stx.enter_context(nc.sbuf_tensor("ymoe" + sfx, [128, 16, 512], F32))
                H["h2"] = stx.enter_context(nc.sbuf_tensor("h2" + sfx, [128, 32768], BF16))
                H["xlnbuf"] = stx.enter_context(nc.sbuf_tensor("xlnbuf" + sfx, [128, 8192], BF16))
                H["ybuf2"] = stx.enter_context(nc.sbuf_tensor("ybuf2" + sfx, [128, 16, 512], BF16))
                H["sqb2"] = stx.enter_context(nc.sbuf_tensor("sqb2" + sfx, [128, 2, 2, 512], BF16))
                H["rl"] = stx.enter_context(nc.sbuf_tensor("rl" + sfx, [128, 2, 512], F32))
                H["upsl"] = stx.enter_context(nc.sbuf_tensor("upsl" + sfx, [128, 2, 16, 128], BF16))
                H["glog"] = stx.enter_context(nc.sbuf_tensor("glog" + sfx, [128, 28], F32))
                H["scl_c"] = stx.enter_context(nc.sbuf_tensor("scl_c" + sfx, [128, 4], F32))
                H["sclb"] = stx.enter_context(nc.sbuf_tensor("sclb" + sfx, [128, 512], F32))
                H["ps0"] = stx.enter_context(nc.psum_tensor("ps0d" + sfx, [128, 512], F32))
                H["ps1"] = stx.enter_context(nc.psum_tensor("ps1d" + sfx, [128, 512], F32))
                H["ps2"] = stx.enter_context(nc.psum_tensor("ps2d" + sfx, [128, 512], F32))
                H["ps3"] = stx.enter_context(nc.psum_tensor("ps3d" + sfx, [128, 512], F32))
                H["psp"] = [stx.enter_context(nc.psum_tensor(f"pspd{i}" + sfx, [128, 512], F32))
                            for i in range(2)]
                H["psg"] = stx.enter_context(nc.psum_tensor("psgd" + sfx, [128, 512], F32))
                H["pss"] = stx.enter_context(nc.psum_tensor("pssd" + sfx, [128, 512], F32))

            with ExitStack() as s4:
                if p2_:
                    ffn_scope(s4, "_a")
                run_block("p4", p47(0))
                barrier()
            if NPH < 6:
                fin("p40.xst.3"); return
            cc_block("xg3", xln3_loc, xg3, "p40.xst.3")

            with ExitStack() as s56:
                if p2_:
                    H["qf"] = s56.enter_context(nc.sbuf_tensor("qf", [128, 4, 8192], BF16))
                    H["kf"] = s56.enter_context(nc.sbuf_tensor("kf", [128, 1, 8192], BF16))
                    H["kftm"] = s56.enter_context(nc.sbuf_tensor("kftm", [128, 64, 128], BF16))
                    H["vtm"] = s56.enter_context(nc.sbuf_tensor("vtm", [128, 64, 128], BF16))
                with ExitStack() as s5:
                    if p2_:
                        H["xbuf5"] = s5.enter_context(nc.sbuf_tensor("xbuf5", [128, 16, 1024], BF16))
                        H["wsl5"] = s5.enter_context(nc.sbuf_tensor("wsl5", [128, 2, 16, 128], BF16))
                        H["cosb"] = s5.enter_context(nc.sbuf_tensor("cosb", [128, 1024], F32))
                        H["scr5"] = s5.enter_context(nc.sbuf_tensor("scr5", [128, 3, 512], F32))
                        H["vbf5"] = s5.enter_context(nc.sbuf_tensor("vbf5", [128, 512], BF16))
                        H["psp"] = [s5.enter_context(nc.psum_tensor(f"pspe{i}", [128, 512], F32))
                                    for i in range(2)]
                        H["pst2"] = [s5.enter_context(nc.psum_tensor(f"pste{i}", [128, 1024], BF16))
                                     for i in range(2)]
                    run_block("p5", [("S", p5_sync), ("P", p5_gpsimd), ("T", p5_tensor),
                                     ("A", p5_scalar), ("V", p5_vector)])
                    em.alias("p5.lastev", f"p5.ev.{NPL5-1}")
                    barrier()
                    if NPH < 7:
                        fin(f"p5.ev.{NPL5-1}"); return
                with ExitStack() as s6:
                    if p2_:
                        H["Sf"] = s6.enter_context(nc.sbuf_tensor("Sf", [128, 128], F32))
                        H["Sbf"] = s6.enter_context(nc.sbuf_tensor("Sbf", [128, 128], BF16))
                        H["zf"] = s6.enter_context(nc.sbuf_tensor("zf", [128, 1], F32))
                        H["zbf"] = s6.enter_context(nc.sbuf_tensor("zbf", [128, 1], BF16))
                        H["zscr"] = s6.enter_context(nc.sbuf_tensor("zscr", [128, 1], F32))
                        H["yf6"] = s6.enter_context(nc.sbuf_tensor("yf6", [128, 512], F32))
                        H["ybb"] = s6.enter_context(nc.sbuf_tensor("ybb", [128, 2, 512], BF16))
                        H["at6"] = s6.enter_context(nc.sbuf_tensor("at6", [128, 2, 512], BF16))
                        H["psy6"] = s6.enter_context(nc.psum_tensor("psy6", [128, 512], F32))
                        H["psd6"] = s6.enter_context(nc.psum_tensor("psd6", [128, 512], F32))
                        H["pss6"] = [s6.enter_context(nc.psum_tensor(f"pss6{i}", [128, 512], F32))
                                     for i in range(2)]
                        H["psb6"] = s6.enter_context(nc.psum_tensor("psb6", [128, 512], F32))
                        H["psS"] = s6.enter_context(nc.psum_tensor("psS", [128, 512], F32))
                    run_block("p6", [("S", p6_sync), ("T", p6_tensor), ("A", p6_scalar),
                                     ("V", p6_vector)])
                    barrier()
            if NPH < 8:
                fin("p6.yst.15.3"); return
            cc_block("ybg", yb_loc, ybg, "p6.yst.15.3")

            with ExitStack() as s7:
                if p2_:
                    ffn_scope(s7, "_b")
                run_block("p7", p47(1))
            fin("p41.ost.3")

    scaffold()
    em.start_pass2(nc, sems)
    bar_n[0] = 0
    scaffold()
    nc._em = em
    return nc


_CACHE = {}


def _get_prog():
    if "nc" not in _CACHE:
        _CACHE["nc"] = build_program()
    return _CACHE["nc"]


def kernel(L0_attn_a_x, L0_attn_a_wq, L0_attn_a_wk, L0_attn_a_wv, L0_attn_a_wo,
           L0_attn_a_lq1, L0_attn_a_lk1, L0_attn_a_lq2, L0_attn_a_lk2,
           L0_attn_b_Wq, L0_attn_b_Wk, L0_attn_b_Wv, L0_attn_b_Wo,
           L0_ffn_gate_w, L0_ffn_up_w, L0_ffn_down_w, L0_ffn_expert_weight,
           L0_ffn_ln_gamma, L0_ffn_ln_beta):
    x = np.asarray(L0_attn_a_x, dtype=np.float32)
    wq = np.asarray(L0_attn_a_wq, np.float32); wk = np.asarray(L0_attn_a_wk, np.float32)
    wv = np.asarray(L0_attn_a_wv, np.float32); wo = np.asarray(L0_attn_a_wo, np.float32)
    Wq = np.asarray(L0_attn_b_Wq, np.float32); Wk = np.asarray(L0_attn_b_Wk, np.float32)
    Wv = np.asarray(L0_attn_b_Wv, np.float32); Wo = np.asarray(L0_attn_b_Wo, np.float32)
    gate = np.ascontiguousarray(np.asarray(L0_ffn_gate_w, np.float32))
    upw_h = _tile_w(np.asarray(L0_ffn_up_w, np.float32))
    dnw_h = _tile_w(np.asarray(L0_ffn_down_w, np.float32))
    exw_h = _tile_w(np.asarray(L0_ffn_expert_weight, np.float32))
    gamc_h = np.ascontiguousarray(np.asarray(L0_ffn_ln_gamma, np.float32).reshape(16, 128).T)
    betc_h = np.ascontiguousarray(np.asarray(L0_ffn_ln_beta, np.float32).reshape(16, 128).T)
    rcos_h, rsin2_h = _rope_tables()
    awoT_h = _tile_w(np.ascontiguousarray(wo.T))
    bwoT_h = _tile_w(np.ascontiguousarray(Wo.T))
    wqT = np.ascontiguousarray(wq.T); wkT = np.ascontiguousarray(wk.T)
    wvT = np.ascontiguousarray(wv.T)
    WqT = np.ascontiguousarray(Wq.T); WkT = np.ascontiguousarray(Wk.T)
    WvT = np.ascontiguousarray(Wv.T)
    lam = [np.ascontiguousarray(np.asarray(v, np.float32).reshape(1, 256))
           for v in (L0_attn_a_lq1, L0_attn_a_lk1, L0_attn_a_lq2, L0_attn_a_lk2)]

    def yidx_h(j):
        a = np.empty((128, 64), np.int32)
        for kc in range(16):
            for tc in range(4):
                rows = (kc // 4) * 2048 + j * 512 + (kc % 4) * 128 + np.arange(128)
                a[:, kc * 4 + tc] = rows * 4 + tc
        return a

    in_maps = []
    for c in range(8):
        b, j = divmod(c, 4)
        in_maps.append({
            "x_sh": np.ascontiguousarray(x[b, j * SH:(j + 1) * SH, :].T),
            "aqT": _tile_w(np.ascontiguousarray(wqT[:, 2 * j * 256:(2 * j + 2) * 256])),
            "akT": _tile_w(np.ascontiguousarray(wkT[:, 2 * j * 256:(2 * j + 2) * 256])),
            "avT": _tile_w(np.ascontiguousarray(wvT[:, 2 * j * 256:(2 * j + 2) * 256])),
            "awoT": awoT_h,
            "lamq1": lam[0], "lamk1": lam[1], "lamq2": lam[2], "lamk2": lam[3],
            "bqT": _tile_w(np.ascontiguousarray(WqT[:, j * 512:(j + 1) * 512])),
            "bkT": _tile_w(np.ascontiguousarray(WkT[:, j * 128:(j + 1) * 128])),
            "bvT": _tile_w(np.ascontiguousarray(WvT[:, j * 128:(j + 1) * 128])),
            "bwoT": bwoT_h,
            "gate_w": gate,
            "upw": upw_h, "dnw": dnw_h, "exw": exw_h,
            "gamc": gamc_h, "betc": betc_h,
            "rcos": rcos_h, "rsin2": rsin2_h,
            "yidx": yidx_h(j),
        })

    nc = _get_prog()
    res = run_bass_kernel_spmd(nc, in_maps, core_ids=list(range(8)))
    _CACHE["last_res"] = res
    out = np.empty((B, T, D), np.float32)
    for c in range(8):
        b, j = divmod(c, 4)
        out[b, j * SH:(j + 1) * SH, :] = res.results[c]["out_fm"].T
    return out
